# version 02 (final)
# baseline (speedup 1.0000x reference)
import numpy as np
import ml_dtypes
from contextlib import ExitStack

import concourse.bass as bass
import concourse.mybir as mybir
from concourse.bass_utils import run_bass_kernel_spmd

F32 = mybir.dt.float32
BF16 = mybir.dt.bfloat16
AF = mybir.ActivationFunctionType
ALU = mybir.AluOpType
AX = mybir.AxisListType.X
BF = ml_dtypes.bfloat16

NCORE = 8
T = 2048
D = 2048
NC = 16
LN_EPS = 1e-5
L0_EPS = 1e-6
RMS_EPS = 1.1920929e-07
SM_SCALE = float(1.0 / np.sqrt(512.0))
W2_SCALE = float(1.0 / np.sqrt(8192.0))
QTH = 0.501953125


class Sem:
    __slots__ = ("h", "n")

    def __init__(self, handle):
        self.h = handle
        self.n = 0


class OpRec:
    __slots__ = ("method", "args", "kwargs", "incs")

    def __init__(self, method, args, kwargs):
        self.method = method
        self.args = args
        self.kwargs = kwargs
        self.incs = []


class EngRec:
    def __init__(self, name):
        self.name = name
        self.ops = []

    def emit(self, method, *args, **kwargs):
        r = OpRec(method, args, kwargs)
        self.ops.append(r)
        return r

    def __getattr__(self, method):
        if method.startswith("_") or method in ("name", "ops", "emit", "replay"):
            raise AttributeError(method)
        return lambda *a, **k: self.emit(method, *a, **k)

    def replay(self, eng):
        for r in self.ops:
            inst = getattr(eng, r.method)(*r.args, **r.kwargs)
            for sem, n in r.incs:
                inst = inst.then_inc(sem.h, n)


class KB:
    def __init__(self, nc, stack):
        self.nc = nc
        self.stack = stack
        self.scopes = [stack]
        self.q = EngRec("q")
        self.t = EngRec("t")
        self.s = EngRec("s")
        self.v = EngRec("v")
        self.g = EngRec("g")
        self.m = {e: Sem(stack.enter_context(nc.semaphore(f"mk_{e}")))
                  for e in ("q", "t", "s", "v", "g")}
        self._last_waited = {}

    def push_scope(self):
        st = ExitStack()
        self.scopes.append(st)

    def pop_scope(self):
        self.scopes.pop().close()

    def sb(self, name, shape, dt=F32):
        return self.scopes[-1].enter_context(
            self.nc.sbuf_tensor("sb_" + name, list(shape), dt))

    def ps(self, name, shape, dt=F32):
        return self.scopes[-1].enter_context(
            self.nc.psum_tensor("ps_" + name, list(shape), dt))

    def mark(self, erec, n=1):
        sem = self.m[erec.name]
        r = erec.ops[-1]
        assert r.method != "wait_ge"
        inc = 16 if r.method in ("dma_start", "dma_start_transpose",
                                 "collective_compute") else n
        r.incs.append((sem, inc))
        sem.n += inc
        return (erec.name, sem.n)

    def wait(self, erec, producer, upto=None):
        if isinstance(producer, tuple):
            pname, val = producer
        else:
            pname = producer.name
            val = self.m[pname].n
        if upto is not None:
            val = upto
        cname = erec.name
        if pname == cname and pname in ("t", "s", "v"):
            return
        if val <= 0:
            return
        key = (cname, pname)
        if self._last_waited.get(key, -1) >= val:
            return
        self._last_waited[key] = val
        erec.emit("wait_ge", self.m[pname].h, val)

    def selfsync(self, erec):
        tok = self.mark(erec)
        erec.emit("wait_ge", self.m[erec.name].h, tok[1])
        return tok

    def dmaq(self, out, in_):
        self.q.emit("dma_start", out=out, in_=in_)
        return self.mark(self.q)

    def dmag(self, out, in_):
        self.g.emit("dma_start", out=out, in_=in_)
        return self.mark(self.g)

    def finalize(self):
        block = self.stack.enter_context(self.nc.Block())

        @block.sync
        def _(e):
            self.q.replay(e)

        @block.tensor
        def _(e):
            self.t.replay(e)

        @block.scalar
        def _(e):
            self.s.replay(e)

        @block.vector
        def _(e):
            self.v.replay(e)

        @block.gpsimd
        def _(e):
            self.g.replay(e)


def rcp(ap, p=128):
    return ap.rearrange("(c p) n -> p c n", p=p)


def _hadamard(n):
    h = np.array([[1.0]], dtype=np.float32)
    while h.shape[0] < n:
        h = np.block([[h, h], [h, -h]])
    return h


def _masks():
    m = np.zeros((4, 128, 512), dtype=np.float32)
    for v in range(4):
        m[v, :, v * 128 + 128:] = -1e9
        blk = m[v, :, v * 128:v * 128 + 128]
        r = np.arange(128)
        blk[r[:, None] < r[None, :]] = -1e9
    return np.ascontiguousarray(m.astype(BF))


def host_prep(inputs):
    g = {}
    g["identb"] = np.eye(128, dtype=np.float32).astype(BF)
    g["identf"] = np.eye(128, dtype=np.float32)
    g["onesf"] = np.ones((1, 128), dtype=np.float32)
    g["onescol_bf"] = np.ones((128, 1), dtype=np.float32).astype(BF)
    g["h64b"] = np.ascontiguousarray(_hadamard(64).astype(BF))
    g["h128b"] = np.ascontiguousarray(_hadamard(128).astype(BF))
    g["masks"] = _masks()
    g["epsln"] = np.full((128, 1), LN_EPS, dtype=np.float32)
    g["epsrms"] = np.full((128, 1), RMS_EPS, dtype=np.float32)
    g["eps0"] = np.full((128, 1), L0_EPS, dtype=np.float32)
    g["WqT0"] = np.ascontiguousarray(np.asarray(inputs["L0_attn_Wq"]).T)
    g["WkT0"] = np.ascontiguousarray(np.asarray(inputs["L0_attn_Wk"]).T)
    g["WvT0"] = np.ascontiguousarray(np.asarray(inputs["L0_attn_Wv"]).T)
    g["WoT0"] = np.ascontiguousarray(np.asarray(inputs["L0_attn_Wo"]).T)
    g["bq0"] = np.ascontiguousarray(np.asarray(inputs["L0_attn_bq"]).reshape(16, 128).T)
    g["bk0"] = np.ascontiguousarray(np.asarray(inputs["L0_attn_bk"]).reshape(1, 512))
    g["bv0"] = np.ascontiguousarray(np.asarray(inputs["L0_attn_bv"]).reshape(1, 512))
    g["W1T"] = np.ascontiguousarray(np.asarray(inputs["L0_ffn_W1"]).T)
    g["W3T"] = np.ascontiguousarray(np.asarray(inputs["L0_ffn_W3"]).T)
    g["W2T"] = np.ascontiguousarray(np.asarray(inputs["L0_ffn_W2"]).T)
    g["Wq1"] = np.ascontiguousarray(np.asarray(inputs["L1_attn_Wq"]))
    g["Wkd"] = np.ascontiguousarray(np.asarray(inputs["L1_attn_Wk_down"]))
    g["Wvd"] = np.ascontiguousarray(np.asarray(inputs["L1_attn_Wv_down"]))
    g["q2lT"] = np.ascontiguousarray(np.asarray(inputs["L1_attn_q_to_latent"]).transpose(0, 2, 1))
    g["vup"] = np.ascontiguousarray(np.asarray(inputs["L1_attn_v_up"]))
    g["Wout"] = np.ascontiguousarray(np.asarray(inputs["L1_attn_Wout"]))
    g["wgu"] = np.ascontiguousarray(np.asarray(inputs["L1_ffn_w_gate_up"]))
    g["wproj"] = np.ascontiguousarray(np.asarray(inputs["L1_ffn_w_proj"]))
    return g


def emit_ln(kb, x, xnT):
    identb = kb.sb("ln_identb", (128, 128), BF16)
    tok_id = kb.dmaq(identb[:], IN["ln_identb"])
    epscol = kb.sb("ln_eps", (128, 1), F32)
    kb.dmaq(epscol[:], IN["ln_eps"])
    xt = kb.sb("ln_xt", (128, 2, D))
    junkf = kb.sb("ln_junkf", (128, D))
    junkb = kb.sb("ln_junkb", (128, D), BF16)
    xn_tok = kb.sb("ln_xntok", (128, 2, D), BF16)
    stg = kb.sb("ln_stg", (128, 2, 16, 128), BF16)
    cols = kb.sb("ln_cols", (128, 8), F32)
    pst = kb.ps("ln_pst", (128, 2, 512), BF16)
    store_toks = []
    for ti in range(16):
        b = ti % 2
        kb.wait(kb.q, kb.v)
        kb.wait(kb.q, kb.s)
        tok_x = kb.dmaq(xt[:, b, :], x[ti * 128:(ti + 1) * 128, :])
        kb.wait(kb.v, tok_x)
        kb.v.tensor_scalar(junkf[:], xt[:, b, :], 1.0, 0.0, ALU.mult,
                           ALU.add, accum_out=cols[:, 0:1])
        kb.selfsync(kb.v)
        kb.wait(kb.s, tok_x)
        kb.s.activation(junkb[:], xt[:, b, :], AF.Square, accum_out=cols[:, 1:2])
        tok_sq = kb.mark(kb.s)
        kb.wait(kb.v, tok_sq)
        kb.v.tensor_scalar(cols[:, 2:3], cols[:, 0:1], 1.0 / D, None, ALU.mult)
        kb.v.tensor_scalar(cols[:, 3:4], cols[:, 1:2], 1.0 / D, None, ALU.mult)
        kb.selfsync(kb.v)
        kb.v.scalar_tensor_tensor(cols[:, 4:5], cols[:, 2:3], cols[:, 2:3],
                                  cols[:, 3:4], ALU.mult, ALU.subtract)
        tok_var = kb.mark(kb.v)
        kb.wait(kb.s, tok_var)
        kb.s.activation(cols[:, 5:6], cols[:, 4:5], AF.Sqrt, scale=-1.0, bias=epscol[:, 0:1])
        tok_std = kb.mark(kb.s)
        kb.wait(kb.v, tok_std)
        kb.v.reciprocal(cols[:, 6:7], cols[:, 5:6])
        kb.selfsync(kb.v)
        kb.wait(kb.v, kb.t)
        kb.v.tensor_scalar(xn_tok[:, b, :], xt[:, b, :], cols[:, 2:3], cols[:, 6:7],
                           ALU.subtract, ALU.mult)
        tok_xn = kb.mark(kb.v)
        kb.wait(kb.t, tok_xn)
        kb.wait(kb.t, tok_id)
        for j in range(16):
            pb = j % 2
            kb.wait(kb.t, kb.s)
            kb.t.transpose(pst[:, pb, 0:128], xn_tok[:, b, j * 128:(j + 1) * 128],
                           identb[:])
            tok_tr = kb.mark(kb.t)
            kb.wait(kb.s, tok_tr)
            if len(store_toks) >= 2:
                kb.wait(kb.s, store_toks[-2])
            kb.s.copy(stg[:, b, j, :], pst[:, pb, 0:128])
            kb.mark(kb.s)
        kb.wait(kb.q, kb.s)
        kb.q.emit("dma_start", out=rcp(xnT)[:, :, ti * 128:(ti + 1) * 128],
                  in_=stg[:, b])
        store_toks.append(kb.mark(kb.q))
    kb.wait(kb.q, store_toks[-1])


def emit_prep(kb, W2T, W1T, W3T, C, W2pT, W1Tb, W3Tb):
    h64 = kb.sb("pp_h64", (64, 64), BF16)
    h128 = kb.sb("pp_h128", (128, 128), BF16)
    t1 = kb.dmaq(h64[:], IN["pp_h64b"])
    t2 = kb.dmaq(h128[:], IN["pp_h128b"])
    lw = kb.sb("pp_lw", (64, 2, D), BF16)
    cw = kb.sb("pp_cw", (64, 2, D), BF16)
    psA = kb.ps("pp_psA", (64, 4, 512), F32)
    W2Tv = W2T.rearrange("(c p) d -> p c d", p=128)
    Cv = C.rearrange("(c p) d -> p c d", p=128)
    ev_toks = []
    st_toks = []
    for p in range(128):
        b = p % 2
        kb.wait(kb.g, kb.t)
        tok_l = kb.dmag(lw[:, b, :], W2Tv[p])
        kb.wait(kb.t, tok_l)
        kb.wait(kb.t, t1)
        if ev_toks:
            kb.wait(kb.t, ev_toks[-1])
        for nb in range(4):
            kb.t.matmul(psA[:, nb, :], h64[:], lw[:, b, nb * 512:(nb + 1) * 512],
                        start=True, stop=True)
        tok_m = kb.mark(kb.t)
        eng = kb.s if p % 2 == 0 else kb.v
        kb.wait(eng, tok_m)
        if len(st_toks) >= 2:
            kb.wait(eng, st_toks[-2])
        if p % 2 == 0:
            eng.copy(cw[:, b, :], psA[:, 0:4, :].rearrange("p a b -> p (a b)"))
        else:
            eng.tensor_copy(cw[:, b, :], psA[:, 0:4, :].rearrange("p a b -> p (a b)"))
        ev_toks.append(kb.mark(eng))
        kb.wait(kb.q, ev_toks[-1])
        kb.q.emit("dma_start", out=Cv[p], in_=cw[:, b, :])
        st_toks.append(kb.mark(kb.q))
    cb = kb.sb("pp_cb", (128, 2, D), BF16)
    wb = kb.sb("pp_wb", (128, 2, D), BF16)
    psB = kb.ps("pp_psB", (128, 4, 512), F32)
    ev2 = []
    st2 = []
    kb.wait(kb.q, st_toks[-1])
    for c in range(64):
        b = c % 2
        kb.wait(kb.q, kb.t)
        tok_l = kb.dmaq(cb[:, b, :], C[c * 128:(c + 1) * 128, :])
        kb.wait(kb.t, tok_l)
        kb.wait(kb.t, t2)
        if ev2:
            kb.wait(kb.t, ev2[-1])
        for nb in range(4):
            kb.t.matmul(psB[:, nb, :], h128[:], cb[:, b, nb * 512:(nb + 1) * 512],
                        start=True, stop=True)
        tok_m = kb.mark(kb.t)
        eng = kb.s if c % 2 == 0 else kb.v
        kb.wait(eng, tok_m)
        if len(st2) >= 2:
            kb.wait(eng, st2[-2])
        for nb in range(4):
            if c % 2 == 0:
                eng.mul(wb[:, b, nb * 512:(nb + 1) * 512], psB[:, nb, :], W2_SCALE)
            else:
                eng.tensor_scalar(wb[:, b, nb * 512:(nb + 1) * 512], psB[:, nb, :],
                                  W2_SCALE, None, ALU.mult)
        ev2.append(kb.mark(eng))
        kb.wait(kb.q, ev2[-1])
        kb.q.emit("dma_start", out=W2pT[c * 128:(c + 1) * 128, :], in_=wb[:, b, :])
        st2.append(kb.mark(kb.q))
    for j in range(16):
        kb.g.emit("dma_start", out=W1Tb[j * 128:(j + 1) * 128, :],
                  in_=W1T[j * 128:(j + 1) * 128, :])
        kb.mark(kb.g)
        kb.g.emit("dma_start", out=W3Tb[j * 128:(j + 1) * 128, :],
                  in_=W3T[j * 128:(j + 1) * 128, :])
        kb.mark(kb.g)
    kb.wait(kb.q, st2[-1])
    kb.wait(kb.q, kb.g)
    kb.wait(kb.q, kb.v)
    kb.wait(kb.q, kb.s)


def emit_attn0(kb, xnT, attT, sT):
    identb = kb.sb("a0_identb", (128, 128), BF16)
    onesf = kb.sb("a0_onesf", (1, 128), F32)
    onescol = kb.sb("a0_onescol", (128, 1), BF16)
    bq = kb.sb("a0_bq", (128, 16), F32)
    bkr = kb.sb("a0_bkr", (1, 512), F32)
    bvr = kb.sb("a0_bvr", (1, 512), F32)
    epsc = kb.sb("a0_eps", (128, 1), F32)
    kb.dmaq(epsc[:], IN["a0_eps"])
    kb.dmaq(identb[:], IN["a0_identb"])
    kb.dmaq(onesf[:], IN["a0_onesf"])
    kb.dmaq(onescol[:], IN["a0_onescol_bf"])
    kb.dmaq(bq[:], IN["a0_bq0"])
    kb.dmaq(bkr[:], IN["a0_bk0"])
    tok_c = kb.dmaq(bvr[:], IN["a0_bv0"])
    ps8 = kb.ps("a0_ps8", (128, 8, 512), F32)
    bkbc = kb.sb("a0_bkbc", (128, 512), F32)
    bvbc = kb.sb("a0_bvbc", (128, 512), F32)
    kb.wait(kb.t, tok_c)
    kb.t.matmul(ps8[:, 0, :], onesf[:], bkr[:], start=True, stop=True)
    kb.t.matmul(ps8[:, 1, :], onesf[:], bvr[:], start=True, stop=True)
    tok_bb = kb.mark(kb.t)
    kb.wait(kb.s, tok_bb)
    kb.s.copy(bkbc[:], ps8[:, 0, :])
    kb.s.copy(bvbc[:], ps8[:, 1, :])
    tok_bbc = kb.mark(kb.s)

    kb.push_scope()
    xn = kb.sb("a0_xn", (128, 16, D), BF16)
    xntoks = [kb.dmaq(xn[:, c, :], xnT[c * 128:(c + 1) * 128, :]) for c in range(NC)]
    kw = kb.sb("a0_kw", (128, 16, 512), BF16)
    vw = kb.sb("a0_vw", (128, 16, 512), BF16)
    for c in range(NC):
        kb.dmag(kw[:, c, :], rcp(IN["a0_WkT0"])[:, c, :])
        kb.dmag(vw[:, c, :], rcp(IN["a0_WvT0"])[:, c, :])
    sk = kb.sb("a0_sk", (128, 16, 512), BF16)
    vv = kb.sb("a0_vv", (128, 16, 512), BF16)
    tmp1 = kb.sb("a0_tmp1", (128, 512), F32)
    tmp2 = kb.sb("a0_tmp2", (128, 512), F32)
    tmp3 = kb.sb("a0_tmp3", (128, 512), F32)
    kb.wait(kb.t, xntoks[-1])
    kb.wait(kb.t, kb.g)
    kb.wait(kb.t, tok_bbc)
    for ti in range(16):
        bk_ = ti % 2
        kb.wait(kb.t, kb.v)
        for c in range(NC):
            kb.t.matmul(ps8[:, bk_, :], xn[:, c, ti * 128:(ti + 1) * 128],
                        kw[:, c, :], start=(c == 0), stop=(c == NC - 1))
        tok_k = kb.mark(kb.t)
        kb.wait(kb.v, tok_k)
        kb.wait(kb.v, kb.s)
        kb.v.tensor_tensor(tmp1[:], ps8[:, bk_, :], bkbc[:], op=ALU.add)
        tok_t1 = kb.mark(kb.v)
        kb.wait(kb.s, tok_t1)
        kb.s.activation(tmp2[:], tmp1[:], AF.Relu, scale=-1.0)
        kb.selfsync(kb.s)
        kb.s.activation(tmp3[:], tmp2[:], AF.Exp, scale=-1.0)
        tok_e = kb.mark(kb.s)
        kb.wait(kb.v, tok_e)
        kb.v.tensor_tensor(tmp1[:], tmp1[:], tmp2[:], op=ALU.add)
        kb.selfsync(kb.v)
        kb.v.tensor_tensor(sk[:, ti, :], tmp1[:], tmp3[:], op=ALU.add)
        kb.mark(kb.v)
        kb.wait(kb.t, kb.v)
        for c in range(NC):
            kb.t.matmul(ps8[:, 1 - bk_, :], xn[:, c, ti * 128:(ti + 1) * 128],
                        vw[:, c, :], start=(c == 0), stop=(c == NC - 1))
        tok_v = kb.mark(kb.t)
        kb.wait(kb.v, tok_v)
        kb.v.tensor_tensor(vv[:, ti, :], ps8[:, 1 - bk_, :], bvbc[:], op=ALU.add)
        kb.mark(kb.v)
    kv = kb.sb("a0_kv", (128, 4, 132), BF16)
    kb.wait(kb.t, kb.v)
    for gidx in range(4):
        bk_ = 2 + gidx % 2
        if gidx >= 2:
            kb.wait(kb.t, kb.s)
        for ti in range(16):
            kb.t.matmul(ps8[:, bk_, 0:128], sk[:, ti, gidx * 128:(gidx + 1) * 128],
                        vv[:, ti, gidx * 128:(gidx + 1) * 128],
                        start=(ti == 0), stop=(ti == 15))
        for ti in range(16):
            kb.t.matmul(ps8[:, bk_, 128:129], sk[:, ti, gidx * 128:(gidx + 1) * 128],
                        onescol[:], start=(ti == 0), stop=(ti == 15))
        tok_kv = kb.mark(kb.t)
        kb.wait(kb.s, tok_kv)
        kb.s.copy(kv[:, gidx, 0:129], ps8[:, bk_, 0:129])
        kb.mark(kb.s)

    wqb = kb.sb("a0_wqb", (128, 2, 16, 128), BF16)
    sq = kb.sb("a0_sq", (128, 4, 512), BF16)
    nr = kb.sb("a0_nr", (1, 2048), F32)
    rb = kb.sb("a0_rb", (128, 4, 512), F32)
    att = kb.sb("a0_att", (128, 2048), BF16)
    att_st = []
    for h in range(16):
        bh = h % 2
        gidx = h // 4
        kb.wait(kb.g, kb.t)
        tok_wq = kb.dmag(wqb[:, bh], rcp(IN["a0_WqT0"])[:, :, h * 128:(h + 1) * 128])
        for tc in range(4):
            bk_ = tc % 2
            kb.wait(kb.t, tok_wq)
            kb.wait(kb.t, kb.v)
            kb.wait(kb.t, kb.s)
            for c in range(NC):
                kb.t.matmul(ps8[:, bk_, :], wqb[:, bh, c, :],
                            xn[:, c, tc * 512:(tc + 1) * 512],
                            start=(c == 0), stop=(c == NC - 1))
            tok_q = kb.mark(kb.t)
            kb.wait(kb.v, tok_q)
            kb.wait(kb.v, kb.s)
            kb.v.tensor_scalar(tmp1[:], ps8[:, bk_, :], bq[:, h:h + 1], None, ALU.add)
            tok_t1 = kb.mark(kb.v)
            kb.wait(kb.s, tok_t1)
            kb.s.activation(tmp2[:], tmp1[:], AF.Relu, scale=-1.0)
            kb.selfsync(kb.s)
            kb.s.activation(tmp3[:], tmp2[:], AF.Exp, scale=-1.0)
            tok_e = kb.mark(kb.s)
            kb.wait(kb.v, tok_e)
            kb.v.tensor_tensor(tmp1[:], tmp1[:], tmp2[:], op=ALU.add)
            kb.selfsync(kb.v)
            kb.v.tensor_tensor(sq[:, tc, :], tmp1[:], tmp3[:], op=ALU.add)
            kb.mark(kb.v)
        for tc in range(4):
            kb.wait(kb.t, kb.v)
            kb.wait(kb.t, kb.s)
            kb.t.matmul(ps8[0:1, 4, tc * 128:tc * 128 + 512]
                        if False else ps8[0:1, 4, 0:512],
                        kv[:, gidx, 128:129], sq[:, tc, :], start=True, stop=True)
            tok_n = kb.mark(kb.t)
            kb.wait(kb.s, tok_n)
            kb.s.activation(nr[0:1, tc * 512:(tc + 1) * 512], ps8[0:1, 4, 0:512],
                            AF.Identity, bias=epsc[0:1, 0:1])
            kb.mark(kb.s)
        for tc in range(4):
            kb.wait(kb.t, kb.s)
            kb.wait(kb.t, kb.v)
            kb.t.matmul(ps8[:, 5, :], onesf[:], nr[0:1, tc * 512:(tc + 1) * 512],
                        start=True, stop=True)
            tok_bcn = kb.mark(kb.t)
            kb.wait(kb.v, tok_bcn)
            kb.v.reciprocal(rb[:, tc, :], ps8[:, 5, :])
            kb.selfsync(kb.v)
        for tc in range(4):
            bk_ = tc % 2
            kb.wait(kb.t, kb.v)
            kb.t.matmul(ps8[:, bk_, :], kv[:, gidx, 0:128],
                        sq[:, tc, :], start=True, stop=True)
            tok_o = kb.mark(kb.t)
            kb.wait(kb.v, tok_o)
            if att_st:
                kb.wait(kb.v, att_st[-1])
            kb.v.tensor_tensor(att[:, tc * 512:(tc + 1) * 512], ps8[:, bk_, :],
                               rb[:, tc, :], op=ALU.mult)
            kb.mark(kb.v)
        kb.wait(kb.q, kb.v)
        kb.q.emit("dma_start", out=attT[h * 128:(h + 1) * 128, :], in_=att[:])
        att_st.append(kb.mark(kb.q))
    kb.pop_scope()
    wo = kb.sb("a0_wo", (128, 16, D), BF16)
    kb.wait(kb.g, kb.t)
    for c in range(NC):
        kb.dmag(wo[:, c, :], rcp(IN["a0_WoT0"])[:, c, :])
    atl = kb.sb("a0_atl", (128, 16, 512), BF16)
    so = kb.sb("a0_so", (128, 2, 512), F32)
    kb.wait(kb.q, att_st[-1])
    for tc in range(4):
        kb.wait(kb.q, kb.t)
        atoks = [kb.dmaq(atl[:, c, :], rcp(attT)[:, c, tc * 512:(tc + 1) * 512])
                 for c in range(NC)]
        kb.wait(kb.t, atoks[-1])
        kb.wait(kb.t, kb.g)
        for dc in range(NC):
            bk_ = dc % 2
            kb.wait(kb.t, kb.s)
            for c in range(NC):
                kb.t.matmul(ps8[:, bk_, :], wo[:, c, dc * 128:(dc + 1) * 128],
                            atl[:, c, :], start=(c == 0), stop=(c == NC - 1))
            tok_w = kb.mark(kb.t)
            kb.wait(kb.s, tok_w)
            kb.wait(kb.s, kb.q)
            kb.s.copy(so[:, bk_, :], ps8[:, bk_, :])
            kb.mark(kb.s)
            kb.wait(kb.q, kb.s)
            kb.q.emit("dma_start", out=sT[dc * 128:(dc + 1) * 128,
                                          tc * 512:(tc + 1) * 512],
                      in_=so[:, bk_, :])
            kb.mark(kb.q)
    kb.wait(kb.q, (kb.q.name, kb.m["q"].n))


def emit_ffn(kb, layer, xnT, sT, xin, xout, W1Tb=None, W3Tb=None, W2pT=None,
             wguT=None, wprojT=None):
    tag = f"f{layer}_"
    identf = kb.sb(tag + "identf", (128, 128), F32)
    kb.dmaq(identf[:], IN[tag + "identf"])
    onesf = kb.sb(tag + "onesf", (1, 128), F32)
    kb.dmaq(onesf[:], IN[tag + "onesf"])
    onescol = kb.sb(tag + "onescol", (128, 1), BF16)
    tok_c = kb.dmaq(onescol[:], IN[tag + "onescol_bf"])
    epscol = kb.sb(tag + "eps", (128, 1), F32)
    kb.dmaq(epscol[:], IN[tag + "eps"])
    nh = 64
    xnc = kb.sb(tag + "xnc", (128, 16, 512), BF16)
    wa = kb.sb(tag + "wa", (128, 2, 16, 128), BF16)
    wb_ = kb.sb(tag + "wb", (128, 2, 16, 128), BF16)
    hid = kb.sb(tag + "hid", (128, 64, 512), BF16)
    su = kb.sb(tag + "su", (128, 512), BF16)
    hsq = kb.sb(tag + "hsq", (128, 512), BF16)
    w2c = kb.sb(tag + "w2c", (128, 2, 64, 128), BF16)
    gbuf = kb.sb(tag + "gbuf", (128, 512), F32)
    sgbuf = kb.sb(tag + "sgbuf", (128, 512), F32)
    sld = kb.sb(tag + "sld", (128, 2, 512), F32)
    asm = kb.sb(tag + "asm", (128, 4, 2048), F32)
    xld = kb.sb(tag + "xld", (128, 2048), F32)
    rrow = kb.sb(tag + "rrow", (1, 512), F32)
    rbc = kb.sb(tag + "rbc", (128, 512), F32)
    psA = kb.ps(tag + "psA", (128, 2, 512), F32)
    psB = kb.ps(tag + "psB", (128, 2, 512), F32)
    psR = kb.ps(tag + "psR", (128, 2, 512), F32)
    for tc in range(4):
        kb.wait(kb.q, kb.t)
        xtoks = [kb.dmaq(xnc[:, c, :], rcp(xnT)[:, c, tc * 512:(tc + 1) * 512])
                 for c in range(NC)]
        for hc in range(nh):
            bh = hc % 2
            wld = kb.g if layer == 0 else kb.q
            kb.wait(wld, kb.t)
            if layer == 0:
                tka = kb.dmag(wa[:, bh], rcp(W1Tb)[:, :, hc * 128:(hc + 1) * 128])
                tkb = kb.dmag(wb_[:, bh], rcp(W3Tb)[:, :, hc * 128:(hc + 1) * 128])
            else:
                tka = kb.dmaq(wa[:, bh], rcp(wguT)[:, :, hc * 128:(hc + 1) * 128])
                tkb = kb.dmaq(wb_[:, bh], rcp(wguT)[:, :, (hc + 64) * 128:(hc + 65) * 128])
            kb.wait(kb.t, xtoks[-1])
            kb.wait(kb.t, tka)
            kb.wait(kb.t, tkb)
            kb.wait(kb.t, kb.s)
            for c in range(NC):
                kb.t.matmul(psA[:, bh, :], wa[:, bh, c, :], xnc[:, c, :],
                            start=(c == 0), stop=(c == NC - 1))
            tok_u = kb.mark(kb.t)
            kb.wait(kb.t, kb.v)
            for c in range(NC):
                kb.t.matmul(psB[:, bh, :], wb_[:, bh, c, :], xnc[:, c, :],
                            start=(c == 0), stop=(c == NC - 1))
            tok_l = kb.mark(kb.t)
            kb.wait(kb.s, tok_u)
            kb.wait(kb.s, kb.v)
            kb.s.activation(su[:], psA[:, bh, :], AF.Silu)
            tok_su = kb.mark(kb.s)
            kb.wait(kb.v, tok_su)
            kb.wait(kb.v, tok_l)
            kb.v.tensor_tensor(hid[:, hc, :], su[:], psB[:, bh, :], op=ALU.mult)
            tok_h = kb.mark(kb.v)
            if layer == 1:
                kb.wait(kb.s, tok_h)
                kb.s.activation(hsq[:], hid[:, hc, :], AF.Square)
                tok_q2 = kb.mark(kb.s)
                kb.wait(kb.t, tok_q2)
                kb.wait(kb.t, tok_c)
                kb.t.matmul(psR[0:1, 0, :], onescol[:], hsq[:],
                            start=(hc == 0), stop=(hc == nh - 1))
                kb.mark(kb.t)
        if layer == 1:
            kb.wait(kb.s, kb.t)
            kb.s.activation(rrow[:], psR[0:1, 0, :], AF.Sqrt,
                            scale=1.0 / 8192.0, bias=epscol[0:1, 0:1])
            tok_sd = kb.mark(kb.s)
            kb.wait(kb.v, tok_sd)
            kb.v.reciprocal(rrow[:], rrow[:])
            tok_ri = kb.selfsync(kb.v)
            kb.wait(kb.t, tok_ri)
            kb.t.matmul(psR[:, 1, :], onesf[:], rrow[:], start=True, stop=True)
            tok_bc = kb.mark(kb.t)
            kb.wait(kb.v, tok_bc)
            kb.v.tensor_copy(rbc[:], psR[:, 1, :])
            tok_rbc = kb.selfsync(kb.v)
            kb.wait(kb.g, tok_rbc)
            for hc in range(nh):
                eng = kb.v if hc % 2 == 0 else kb.g
                eng.tensor_tensor(hid[:, hc, :], hid[:, hc, :], rbc[:], op=ALU.mult)
                kb.mark(eng)
        kb.wait(kb.t, kb.v)
        kb.wait(kb.t, kb.g)
        for dc in range(NC):
            bd = dc % 2
            kb.wait(kb.q, kb.t)
            kb.wait(kb.q, kb.v)
            if layer == 0:
                tok_w2 = kb.dmaq(w2c[:, bd], rcp(W2pT)[:, :, dc * 128:(dc + 1) * 128])
            else:
                tok_w2 = kb.dmaq(w2c[:, bd], rcp(wprojT)[:, :, dc * 128:(dc + 1) * 128])
            tok_s = kb.dmaq(sld[:, bd, :], sT[dc * 128:(dc + 1) * 128,
                                              tc * 512:(tc + 1) * 512])
            kb.wait(kb.t, tok_w2)
            kb.wait(kb.t, kb.s)
            for hc in range(nh):
                kb.t.matmul(psA[:, bd, :], w2c[:, bd, hc, :], hid[:, hc, :],
                            start=(hc == 0), stop=(hc == nh - 1))
            tok_o = kb.mark(kb.t)
            kb.wait(kb.s, tok_o)
            kb.wait(kb.s, kb.v)
            kb.s.activation(gbuf[:], psA[:, bd, :], AF.Silu)
            tok_g = kb.mark(kb.s)
            kb.wait(kb.v, tok_g)
            kb.wait(kb.v, tok_s)
            kb.wait(kb.v, kb.t)
            kb.v.tensor_tensor(sgbuf[:], gbuf[:], sld[:, bd, :], op=ALU.mult)
            tok_sg = kb.mark(kb.v)
            kb.wait(kb.t, tok_sg)
            kb.wait(kb.t, kb.q)
            for sub in range(4):
                pb = sub % 2
                kb.wait(kb.t, kb.v)
                kb.t.transpose(psB[:, pb, 0:128], sgbuf[:, sub * 128:(sub + 1) * 128],
                               identf[:])
                tok_tr = kb.mark(kb.t)
                kb.wait(kb.v, tok_tr)
                kb.wait(kb.v, kb.g)
                kb.wait(kb.v, kb.q)
                kb.v.tensor_copy(asm[:, sub, dc * 128:(dc + 1) * 128], psB[:, pb, 0:128])
                kb.mark(kb.v)
        for sub in range(4):
            kb.wait(kb.q, kb.g)
            tok_x = kb.dmaq(xld[:], xin[tc * 512 + sub * 128: tc * 512 + (sub + 1) * 128, :])
            kb.wait(kb.g, tok_x)
            kb.wait(kb.g, kb.v)
            kb.g.tensor_tensor(asm[:, sub, :], asm[:, sub, :], xld[:], op=ALU.add)
            kb.mark(kb.g)
            kb.wait(kb.q, kb.g)
            kb.q.emit("dma_start",
                      out=xout[tc * 512 + sub * 128: tc * 512 + (sub + 1) * 128, :],
                      in_=asm[:, sub, :])
            kb.mark(kb.q)
    kb.wait(kb.q, (kb.q.name, kb.m["q"].n))


def emit_quant(kb, wgu_sl, wproj_sl, gu_shard, proj_shard):
    identb = kb.sb("qt_identb", (128, 128), BF16)
    tok_id = kb.dmaq(identb[:], IN["qt_identb"])
    pst = kb.ps("qt_pst", (128, 2, 512), BF16)

    def one(tag, wsl, out_shard, ndout, ncols):
        ngr = ncols // 64
        nkt = ncols // 128
        wn = kb.sb(tag + "wn", (128, 2, ncols), BF16)
        srow = kb.sb(tag + "srow", (128, 2, 4 * ngr), F32)
        meanb = kb.sb(tag + "meanb", (128, ngr), BF16)
        what = kb.sb(tag + "what", (128, 2, ncols), BF16)
        g1 = kb.sb(tag + "g1", (128, ncols), BF16)
        g2 = kb.sb(tag + "g2", (128, ncols), BF16)
        stg = kb.sb(tag + "stg", (128, 2, nkt, 128), BF16)
        st_toks = []
        for dc in range(ndout):
            b = dc % 2
            kb.wait(kb.g, kb.v)
            tok_w = kb.dmag(wn[:, b, :], wsl[dc * 128:(dc + 1) * 128, :])
            kb.wait(kb.v, tok_w)
            kb.wait(kb.v, kb.t)
            sr = srow[:, b, :]
            wv = wn[:, b, :].rearrange("p (g e) -> p g e", e=64)
            kb.v.tensor_reduce(sr[:, 0:ngr], wv, axis=AX, op=ALU.add,
                               apply_absolute_value=True)
            kb.selfsync(kb.v)
            kb.v.tensor_scalar(meanb[:], sr[:, 0:ngr], 1.0 / 64.0, None, ALU.mult)
            kb.selfsync(kb.v)
            kb.v.tensor_scalar(sr[:, ngr:2 * ngr], meanb[:], 1e-8, None, ALU.max)
            kb.selfsync(kb.v)
            kb.v.tensor_scalar(sr[:, 2 * ngr:3 * ngr], sr[:, ngr:2 * ngr],
                               QTH, None, ALU.mult)
            kb.selfsync(kb.v)
            kb.v.tensor_scalar(sr[:, 3 * ngr:4 * ngr], sr[:, 2 * ngr:3 * ngr],
                               -1.0, None, ALU.mult)
            tok_r = kb.selfsync(kb.v)
            thb = sr[:, 2 * ngr:3 * ngr].unsqueeze(2).broadcast_to((128, ngr, 64))
            nthb = sr[:, 3 * ngr:4 * ngr].unsqueeze(2).broadcast_to((128, ngr, 64))
            sbv = sr[:, ngr:2 * ngr].unsqueeze(2).broadcast_to((128, ngr, 64))
            kb.v.tensor_tensor(g1[:].rearrange("p (g e) -> p g e", e=64), wv, thb,
                               op=ALU.is_gt)
            kb.v.tensor_tensor(g2[:].rearrange("p (g e) -> p g e", e=64), wv, nthb,
                               op=ALU.is_lt)
            kb.selfsync(kb.v)
            kb.v.tensor_tensor(g1[:], g1[:], g2[:], op=ALU.subtract)
            kb.selfsync(kb.v)
            kb.v.tensor_tensor(what[:, b, :].rearrange("p (g e) -> p g e", e=64),
                               g1[:].rearrange("p (g e) -> p g e", e=64), sbv,
                               op=ALU.mult)
            tok_wh = kb.mark(kb.v)
            kb.wait(kb.t, tok_wh)
            kb.wait(kb.t, tok_id)
            for kt in range(nkt):
                pb = kt % 2
                kb.wait(kb.t, kb.s)
                kb.t.transpose(pst[:, pb, 0:128],
                               what[:, b, kt * 128:(kt + 1) * 128], identb[:])
                tok_tr = kb.mark(kb.t)
                kb.wait(kb.s, tok_tr)
                if len(st_toks) >= 2:
                    kb.wait(kb.s, st_toks[-2])
                kb.s.copy(stg[:, b, kt, :], pst[:, pb, 0:128])
                kb.mark(kb.s)
            kb.wait(kb.q, kb.s)
            kb.q.emit("dma_start",
                      out=out_shard.rearrange("(kt p) d -> p kt d", p=128)[:, :, dc * 128:(dc + 1) * 128],
                      in_=stg[:, b])
            st_toks.append(kb.mark(kb.q))
        kb.wait(kb.q, st_toks[-1])

    one("qg_", wgu_sl, gu_shard, 128, 256)
    one("qp_", wproj_sl, proj_shard, 16, 1024)


def emit_attn1(kb, xnT, attT, sT):
    identb = kb.sb("a1_identb", (128, 128), BF16)
    kb.dmaq(identb[:], IN["a1_identb"])
    masks = kb.sb("a1_masks", (128, 4, 512), BF16)
    tok_mk = kb.dmaq(masks[:], IN["a1_masks"].rearrange("m p n -> p m n"))
    psA = kb.ps("a1_psA", (128, 4, 512), F32)
    psB = kb.ps("a1_psB", (128, 2, 512), F32)
    pst = kb.ps("a1_pst", (128, 2, 512), BF16)
    kb.push_scope()
    xn = kb.sb("a1_xn", (128, 16, D), BF16)
    xntoks = [kb.dmaq(xn[:, c, :], xnT[c * 128:(c + 1) * 128, :]) for c in range(NC)]
    kwd = kb.sb("a1_kwd", (128, 16, 128), BF16)
    kl = kb.sb("a1_kl", (128, 4, 2048), BF16)
    vl = kb.sb("a1_vl", (128, 4, 2048), BF16)
    for dst, w in ((kl, "a1_Wkd"), (vl, "a1_Wvd")):
        for lc in range(4):
            kb.wait(kb.g, kb.t)
            tok_w = kb.dmag(kwd[:], rcp(IN[w])[:, :, lc * 128:(lc + 1) * 128])
            for tc in range(4):
                bk_ = tc % 2
                kb.wait(kb.t, tok_w)
                kb.wait(kb.t, xntoks[-1])
                kb.wait(kb.t, kb.s)
                for c in range(NC):
                    kb.t.matmul(psB[:, bk_, :], kwd[:, c, :],
                                xn[:, c, tc * 512:(tc + 1) * 512],
                                start=(c == 0), stop=(c == NC - 1))
                tok_m = kb.mark(kb.t)
                kb.wait(kb.s, tok_m)
                kb.s.copy(dst[:, lc, tc * 512:(tc + 1) * 512], psB[:, bk_, :])
                kb.mark(kb.s)
    wq1 = kb.sb("a1_wq1", (128, 2, 16, 128), BF16)
    q2l = kb.sb("a1_q2l", (128, 2, 4, 128), BF16)
    vupw = kb.sb("a1_vupw", (128, 2, 4, 128), BF16)
    qh = kb.sb("a1_qh", (128, 2048), BF16)
    ke = kb.sb("a1_ke", (128, 2048), BF16)
    vu = kb.sb("a1_vu", (128, 16, 128), BF16)
    ptile = kb.sb("a1_pt", (128, 4, 512), BF16)
    ptt = kb.sb("a1_ptt", (128, 16, 128), BF16)
    oh = kb.sb("a1_oh", (128, 2048), BF16)
    mxb = kb.sb("a1_mxb", (128, 12), F32)
    att_st = []
    for h in range(16):
        bh = h % 2
        kb.wait(kb.g, kb.t)
        tok_wq = kb.dmag(wq1[:, bh], rcp(IN["a1_Wq1"])[:, :, h * 128:(h + 1) * 128])
        tok_q2 = kb.dmag(q2l[:, bh], IN["a1_q2lT"][h].rearrange("(lc p) d -> p lc d", p=128))
        tok_vw = kb.dmag(vupw[:, bh], IN["a1_vup"][h].rearrange("(lc p) d -> p lc d", p=128))
        for tc in range(4):
            bk_ = tc % 2
            kb.wait(kb.t, tok_wq)
            kb.wait(kb.t, kb.s)
            for c in range(NC):
                kb.t.matmul(psB[:, bk_, :], wq1[:, bh, c, :],
                            xn[:, c, tc * 512:(tc + 1) * 512],
                            start=(c == 0), stop=(c == NC - 1))
            tok_m = kb.mark(kb.t)
            kb.wait(kb.s, tok_m)
            kb.s.mul(qh[:, tc * 512:(tc + 1) * 512], psB[:, bk_, :], SM_SCALE)
            kb.mark(kb.s)
        for tc in range(4):
            bk_ = tc % 2
            kb.wait(kb.t, tok_q2)
            kb.wait(kb.t, kb.s)
            for lc in range(4):
                kb.t.matmul(psB[:, bk_, :], q2l[:, bh, lc, :],
                            kl[:, lc, tc * 512:(tc + 1) * 512],
                            start=(lc == 0), stop=(lc == 3))
            tok_m = kb.mark(kb.t)
            kb.wait(kb.s, tok_m)
            kb.s.copy(ke[:, tc * 512:(tc + 1) * 512], psB[:, bk_, :])
            kb.mark(kb.s)
        for ti in range(16):
            bk_ = ti % 2
            kb.wait(kb.t, tok_vw)
            kb.wait(kb.t, kb.s)
            for lc in range(4):
                kb.t.matmul(psB[:, bk_, 0:128], vl[:, lc, ti * 128:(ti + 1) * 128],
                            vupw[:, bh, lc, :], start=(lc == 0), stop=(lc == 3))
            tok_m = kb.mark(kb.t)
            kb.wait(kb.s, tok_m)
            kb.s.copy(vu[:, ti, :], psB[:, bk_, 0:128])
            kb.mark(kb.s)
        for qi in range(16):
            nvalid = (qi + 1) * 128
            ng = (nvalid + 511) // 512
            kb.wait(kb.t, kb.s)
            kb.wait(kb.t, kb.v)
            for gk in range(ng):
                kb.t.matmul(psA[:, gk, :], qh[:, qi * 128:(qi + 1) * 128],
                            ke[:, gk * 512:(gk + 1) * 512], start=True, stop=True)
            tok_sc = kb.mark(kb.t)
            kb.wait(kb.v, tok_sc)
            kb.wait(kb.v, tok_mk)
            kb.v.tensor_tensor(psA[:, ng - 1, :], psA[:, ng - 1, :],
                               masks[:, qi % 4, :], op=ALU.add)
            kb.selfsync(kb.v)
            for gk in range(ng):
                kb.v.reduce_max(mxb[:, gk:gk + 1], psA[:, gk, :], axis=AX, negate=True)
            kb.selfsync(kb.v)
            kb.v.tensor_reduce(mxb[:, 4:5], mxb[:, 0:ng], axis=AX, op=ALU.min)
            tok_nm = kb.mark(kb.v)
            kb.wait(kb.s, tok_nm)
            kb.wait(kb.s, kb.t)
            for gk in range(ng):
                kb.s.activation(ptile[:, gk, :], psA[:, gk, :], AF.Exp,
                                bias=mxb[:, 4:5], accum_out=mxb[:, 5 + gk:6 + gk])
            tok_p = kb.mark(kb.s)
            kb.wait(kb.v, tok_p)
            kb.v.tensor_reduce(mxb[:, 9:10], mxb[:, 5:5 + ng], axis=AX, op=ALU.add)
            kb.selfsync(kb.v)
            kb.v.reciprocal(mxb[:, 10:11], mxb[:, 9:10])
            kb.selfsync(kb.v)
            for gk in range(ng):
                kb.v.tensor_scalar(ptile[:, gk, :], ptile[:, gk, :],
                                   mxb[:, 10:11], None, ALU.mult)
            tok_ps = kb.mark(kb.v)
            kb.wait(kb.t, tok_ps)
            for kt in range(qi + 1):
                pb = kt % 2
                kb.wait(kb.t, kb.s)
                kb.t.transpose(pst[:, pb, 0:128],
                               ptile[:, kt // 4, (kt % 4) * 128:(kt % 4 + 1) * 128],
                               identb[:])
                tok_tr = kb.mark(kb.t)
                kb.wait(kb.s, tok_tr)
                kb.s.copy(ptt[:, kt, :], pst[:, pb, 0:128])
                kb.mark(kb.s)
            kb.wait(kb.t, kb.s)
            for kt in range(qi + 1):
                kb.t.matmul(psB[:, 0, 0:128], vu[:, kt, :], ptt[:, kt, :],
                            start=(kt == 0), stop=(kt == qi))
            tok_o = kb.mark(kb.t)
            kb.wait(kb.s, tok_o)
            kb.s.copy(oh[:, qi * 128:(qi + 1) * 128], psB[:, 0, 0:128])
            kb.mark(kb.s)
        kb.wait(kb.q, kb.s)
        kb.q.emit("dma_start", out=attT[h * 128:(h + 1) * 128, :], in_=oh[:])
        att_st.append(kb.mark(kb.q))
    kb.pop_scope()
    wo = kb.sb("a1_wo", (128, 16, D), BF16)
    kb.wait(kb.g, kb.t)
    for c in range(NC):
        kb.dmag(wo[:, c, :], rcp(IN["a1_Wout"])[:, c, :])
    atl = kb.sb("a1_atl", (128, 16, 512), BF16)
    so = kb.sb("a1_so", (128, 2, 512), F32)
    kb.wait(kb.q, att_st[-1])
    for tc in range(4):
        kb.wait(kb.q, kb.t)
        atoks = [kb.dmaq(atl[:, c, :], rcp(attT)[:, c, tc * 512:(tc + 1) * 512])
                 for c in range(NC)]
        kb.wait(kb.t, atoks[-1])
        kb.wait(kb.t, kb.g)
        for dc in range(NC):
            bk_ = dc % 2
            kb.wait(kb.t, kb.s)
            for c in range(NC):
                kb.t.matmul(psB[:, bk_, :], wo[:, c, dc * 128:(dc + 1) * 128],
                            atl[:, c, :], start=(c == 0), stop=(c == NC - 1))
            tok_w = kb.mark(kb.t)
            kb.wait(kb.s, tok_w)
            kb.wait(kb.s, kb.q)
            kb.s.copy(so[:, bk_, :], psB[:, bk_, :])
            kb.mark(kb.s)
            kb.wait(kb.q, kb.s)
            kb.q.emit("dma_start", out=sT[dc * 128:(dc + 1) * 128,
                                          tc * 512:(tc + 1) * 512],
                      in_=so[:, bk_, :])
            kb.mark(kb.q)
    kb.wait(kb.q, (kb.q.name, kb.m["q"].n))


IN = {}


def _declare(nc, specs):
    IN.clear()
    aps = {}
    for name, shape, dt, kind in specs:
        ap = nc.dram_tensor(name, list(shape), dt, kind=kind).ap()
        aps[name] = ap
        IN[name] = ap
    return aps


def _run(build_fn, in_maps):
    nc, outs = build_fn()
    res = run_bass_kernel_spmd(nc, in_maps, core_ids=list(range(NCORE)))
    return [{k: np.asarray(r[k]) for k in outs} for r in res.results]


def _build_ln():
    nc = bass.Bass("TRN2", target_bir_lowering=False)
    aps = _declare(nc, [
        ("x", (T, D), F32, "ExternalInput"),
        ("ln_identb", (128, 128), BF16, "ExternalInput"),
        ("ln_eps", (128, 1), F32, "ExternalInput"),
        ("xnT", (T, D), BF16, "ExternalOutput"),
    ])
    with ExitStack() as stack:
        kb = KB(nc, stack)
        emit_ln(kb, aps["x"], aps["xnT"])
        kb.finalize()
    return nc, ["xnT"]


def _build_prep():
    nc = bass.Bass("TRN2", target_bir_lowering=False)
    aps = _declare(nc, [
        ("pp_W2T", (8192, D), F32, "ExternalInput"),
        ("pp_W1T", (D, 8192), F32, "ExternalInput"),
        ("pp_W3T", (D, 8192), F32, "ExternalInput"),
        ("pp_h64b", (64, 64), BF16, "ExternalInput"),
        ("pp_h128b", (128, 128), BF16, "ExternalInput"),
        ("pp_C", (8192, D), BF16, "Internal"),
        ("pp_W2pT", (8192, D), BF16, "ExternalOutput"),
        ("pp_W1Tb", (D, 8192), BF16, "ExternalOutput"),
        ("pp_W3Tb", (D, 8192), BF16, "ExternalOutput"),
    ])
    with ExitStack() as stack:
        kb = KB(nc, stack)
        emit_prep(kb, aps["pp_W2T"], aps["pp_W1T"], aps["pp_W3T"], aps["pp_C"],
                  aps["pp_W2pT"], aps["pp_W1Tb"], aps["pp_W3Tb"])
        kb.finalize()
    return nc, ["pp_W2pT", "pp_W1Tb", "pp_W3Tb"]


def _build_attn0():
    nc = bass.Bass("TRN2", target_bir_lowering=False)
    aps = _declare(nc, [
        ("xnT", (T, D), BF16, "ExternalInput"),
        ("a0_identb", (128, 128), BF16, "ExternalInput"),
        ("a0_onesf", (1, 128), F32, "ExternalInput"),
        ("a0_onescol_bf", (128, 1), BF16, "ExternalInput"),
        ("a0_bq0", (128, 16), F32, "ExternalInput"),
        ("a0_bk0", (1, 512), F32, "ExternalInput"),
        ("a0_bv0", (1, 512), F32, "ExternalInput"),
        ("a0_eps", (128, 1), F32, "ExternalInput"),
        ("a0_WqT0", (D, D), F32, "ExternalInput"),
        ("a0_WkT0", (D, 512), F32, "ExternalInput"),
        ("a0_WvT0", (D, 512), F32, "ExternalInput"),
        ("a0_WoT0", (D, D), F32, "ExternalInput"),
        ("attT", (D, T), BF16, "Internal"),
        ("sT", (D, T), F32, "ExternalOutput"),
    ])
    with ExitStack() as stack:
        kb = KB(nc, stack)
        emit_attn0(kb, aps["xnT"], aps["attT"], aps["sT"])
        kb.finalize()
    return nc, ["sT"]


def _build_ffn(layer):
    def f():
        nc = bass.Bass("TRN2", target_bir_lowering=False)
        tag = f"f{layer}_"
        specs = [
            ("xnT", (T, D), BF16, "ExternalInput"),
            ("sT", (D, T), F32, "ExternalInput"),
            ("xin", (T, D), F32, "ExternalInput"),
            (tag + "identf", (128, 128), F32, "ExternalInput"),
            (tag + "onesf", (1, 128), F32, "ExternalInput"),
            (tag + "onescol_bf", (128, 1), BF16, "ExternalInput"),
            (tag + "eps", (128, 1), F32, "ExternalInput"),
            ("xout", (T, D), F32, "ExternalOutput"),
        ]
        if layer == 0:
            specs += [("pp_W1Tb", (D, 8192), BF16, "ExternalInput"),
                      ("pp_W3Tb", (D, 8192), BF16, "ExternalInput"),
                      ("pp_W2pT", (8192, D), BF16, "ExternalInput")]
        else:
            specs += [("wguT", (D, 16384), BF16, "ExternalInput"),
                      ("wprojT", (8192, D), BF16, "ExternalInput")]
        aps = _declare(nc, specs)
        with ExitStack() as stack:
            kb = KB(nc, stack)
            if layer == 0:
                emit_ffn(kb, 0, aps["xnT"], aps["sT"], aps["xin"], aps["xout"],
                         W1Tb=aps["pp_W1Tb"], W3Tb=aps["pp_W3Tb"], W2pT=aps["pp_W2pT"])
            else:
                emit_ffn(kb, 1, aps["xnT"], aps["sT"], aps["xin"], aps["xout"],
                         wguT=aps["wguT"], wprojT=aps["wprojT"])
            kb.finalize()
        return nc, ["xout"]
    return f


def _build_quant():
    nc = bass.Bass("TRN2", target_bir_lowering=False)
    aps = _declare(nc, [
        ("qt_wgu_sl", (16384, 256), F32, "ExternalInput"),
        ("qt_wproj_sl", (D, 1024), F32, "ExternalInput"),
        ("qt_identb", (128, 128), BF16, "ExternalInput"),
        ("qt_gu_shard", (256, 16384), BF16, "ExternalOutput"),
        ("qt_proj_shard", (1024, D), BF16, "ExternalOutput"),
    ])
    with ExitStack() as stack:
        kb = KB(nc, stack)
        emit_quant(kb, aps["qt_wgu_sl"], aps["qt_wproj_sl"],
                   aps["qt_gu_shard"], aps["qt_proj_shard"])
        kb.finalize()
    return nc, ["qt_gu_shard", "qt_proj_shard"]


def _build_attn1(dump=False):
    nc = bass.Bass("TRN2", target_bir_lowering=False)
    aps = _declare(nc, [
        ("xnT", (T, D), BF16, "ExternalInput"),
        ("a1_identb", (128, 128), BF16, "ExternalInput"),
        ("a1_masks", (4, 128, 512), BF16, "ExternalInput"),
        ("a1_Wq1", (D, D), F32, "ExternalInput"),
        ("a1_Wkd", (D, 512), F32, "ExternalInput"),
        ("a1_Wvd", (D, 512), F32, "ExternalInput"),
        ("a1_q2lT", (16, 512, 128), F32, "ExternalInput"),
        ("a1_vup", (16, 512, 128), F32, "ExternalInput"),
        ("a1_Wout", (D, D), F32, "ExternalInput"),
        ("attT", (D, T), BF16, "ExternalOutput" if dump else "Internal"),
        ("sT", (D, T), F32, "ExternalOutput"),
    ])
    with ExitStack() as stack:
        kb = KB(nc, stack)
        emit_attn1(kb, aps["xnT"], aps["attT"], aps["sT"])
        kb.finalize()
    return nc, (["sT", "attT"] if dump else ["sT"])


def kernel(**inputs):
    g = host_prep(inputs)
    x_full = np.ascontiguousarray(np.asarray(inputs["L0_attn_hidden_states"],
                                             dtype=np.float32))
    xs = [np.ascontiguousarray(x_full[b]) for b in range(NCORE)]

    maps = [dict(x=xs[b], ln_identb=g["identb"], ln_eps=g["epsln"]) for b in range(NCORE)]
    r = _run(_build_ln, maps)
    xn0 = [r[b]["xnT"] for b in range(NCORE)]

    maps = [dict(xnT=xn0[b], a0_identb=g["identb"], a0_onesf=g["onesf"],
                 a0_onescol_bf=g["onescol_bf"], a0_bq0=g["bq0"], a0_bk0=g["bk0"],
                 a0_bv0=g["bv0"], a0_eps=g["eps0"], a0_WqT0=g["WqT0"], a0_WkT0=g["WkT0"],
                 a0_WvT0=g["WvT0"], a0_WoT0=g["WoT0"]) for b in range(NCORE)]
    r = _run(_build_attn0, maps)
    s0 = [r[b]["sT"] for b in range(NCORE)]

    maps = [dict(pp_W2T=g["W2T"], pp_W1T=g["W1T"], pp_W3T=g["W3T"],
                 pp_h64b=g["h64b"], pp_h128b=g["h128b"]) for b in range(NCORE)]
    r = _run(_build_prep, maps)
    w2p = [r[b]["pp_W2pT"] for b in range(NCORE)]
    w1b = [r[b]["pp_W1Tb"] for b in range(NCORE)]
    w3b = [r[b]["pp_W3Tb"] for b in range(NCORE)]

    maps = [dict(xnT=xn0[b], sT=s0[b], xin=xs[b], f0_identf=g["identf"],
                 f0_onesf=g["onesf"], f0_onescol_bf=g["onescol_bf"], f0_eps=g["epsrms"],
                 pp_W1Tb=w1b[b], pp_W3Tb=w3b[b], pp_W2pT=w2p[b])
            for b in range(NCORE)]
    r = _run(_build_ffn(0), maps)
    x1 = [r[b]["xout"] for b in range(NCORE)]

    maps = [dict(x=x1[b], ln_identb=g["identb"], ln_eps=g["epsln"]) for b in range(NCORE)]
    r = _run(_build_ln, maps)
    xn1 = [r[b]["xnT"] for b in range(NCORE)]

    maps = [dict(qt_wgu_sl=np.ascontiguousarray(g["wgu"][:, b * 256:(b + 1) * 256]),
                 qt_wproj_sl=np.ascontiguousarray(g["wproj"][:, b * 1024:(b + 1) * 1024]),
                 qt_identb=g["identb"]) for b in range(NCORE)]
    r = _run(_build_quant, maps)
    wguT = np.ascontiguousarray(np.concatenate([r[b]["qt_gu_shard"] for b in range(NCORE)], axis=0))
    wprojT = np.ascontiguousarray(np.concatenate([r[b]["qt_proj_shard"] for b in range(NCORE)], axis=0))

    maps = [dict(xnT=xn1[b], a1_identb=g["identb"], a1_masks=g["masks"],
                 a1_Wq1=g["Wq1"], a1_Wkd=g["Wkd"], a1_Wvd=g["Wvd"],
                 a1_q2lT=g["q2lT"], a1_vup=g["vup"], a1_Wout=g["Wout"])
            for b in range(NCORE)]
    r = _run(_build_attn1, maps)
    s1 = [r[b]["sT"] for b in range(NCORE)]

    maps = [dict(xnT=xn1[b], sT=s1[b], xin=x1[b], f1_identf=g["identf"],
                 f1_onesf=g["onesf"], f1_onescol_bf=g["onescol_bf"], f1_eps=g["epsrms"],
                 wguT=wguT, wprojT=wprojT) for b in range(NCORE)]
    r = _run(_build_ffn(1), maps)
    out = np.stack([r[b]["xout"] for b in range(NCORE)], axis=0)
    return out.astype(np.float32)
